# version 06 (final)
# baseline (speedup 1.0000x reference)
import contextlib
import os
import numpy as np
import ml_dtypes
import concourse.bass as bass
import concourse.mybir as mybir
from concourse.bass_utils import run_bass_kernel_spmd

F32 = mybir.dt.float32
BF16 = mybir.dt.bfloat16
I32 = mybir.dt.int32
AF = mybir.ActivationFunctionType
ALU = mybir.AluOpType
AX = mybir.AxisListType

B, S, H, IF = 2, 4096, 2048, 8192
NCORE = 8
NTOK = 1024
HC = H // 128
I2C = IF // 128
EPS = 1e-5
WN = float(H * IF)
SHARD_N = H * IF // NCORE
INV_SQRT_D = 1.0 / float(np.sqrt(128.0))

MILESTONE = 2
DETECT_RACES = False
KSTAGE = int(os.environ.get("KSTAGE", "99"))


class Sem:
    def __init__(self, ctx, nc, name):
        self.h = ctx.enter_context(nc.semaphore(name))
        self.n = 0

    def inc(self, instr, k=1):
        instr.then_inc(self.h, k)
        self.n += k
        return self.n


class SlotSems:
    def __init__(self, ctx, nc, name, nslots=2):
        self.s = [Sem(ctx, nc, "%s%d" % (name, i)) for i in range(nslots)]

    def inc16(self, instr, slot):
        s = self.s[slot]
        instr.then_inc(s.h, 16)
        s.n += 16
        return s.n

    def count(self, slot):
        return self.s[slot].n

    def h(self, slot):
        return self.s[slot].h


def build():
    nc = bass.Bass("TRN2", target_bir_lowering=False, detect_race_conditions=DETECT_RACES)

    xin = nc.dram_tensor("xin", [NTOK, H], F32, kind="ExternalInput").ap()
    wshg = nc.dram_tensor("wshg", [H, 4 * 256], F32, kind="ExternalInput").ap()
    wshu = nc.dram_tensor("wshu", [H, 4 * 256], F32, kind="ExternalInput").ap()
    wshd = nc.dram_tensor("wshd", [8 * 128, H], F32, kind="ExternalInput").ap()
    ident_in = nc.dram_tensor("ident", [128, 128], BF16, kind="ExternalInput").ap()
    wq_s = nc.dram_tensor("wq_s", [H, 512], F32, kind="ExternalInput").ap()
    wk_s = nc.dram_tensor("wk_s", [H, 128], F32, kind="ExternalInput").ap()
    wv_s = nc.dram_tensor("wv_s", [H, 128], F32, kind="ExternalInput").ap()
    wd_s = nc.dram_tensor("wd_s", [512, H], F32, kind="ExternalInput").ap()
    masks_in = nc.dram_tensor("masks", [256, 1024], BF16, kind="ExternalInput").ap()
    yout = nc.dram_tensor("yout", [NTOK, H], F32, kind="ExternalOutput").ap()
    dbg = nc.dram_tensor("dbg", [128, 64], F32, kind="ExternalOutput").ap()
    dbg2 = nc.dram_tensor("dbg2", [128, 4096], F32, kind="ExternalOutput").ap()

    gu_in = [nc.dram_tensor("gu_in%d" % j, [128, IF], BF16).ap() for j in range(4)]
    stash_gu = nc.dram_tensor("stash_gu", [32 * 128, IF], BF16).ap()
    d_in = [nc.dram_tensor("d_in%d" % j, [4 * 128, H], BF16).ap() for j in range(2)]
    stash_d = nc.dram_tensor("stash_d", [I2C * 128, H], BF16).ap()
    ws_in = nc.dram_tensor("ws_in", [128, 4], F32).ap()
    ws_out = nc.dram_tensor("ws_out", [128, 4], F32).ap()
    z2t_in = nc.dram_tensor("z2t_in", [H, NTOK], BF16).ap()
    z2t_out = nc.dram_tensor("z2t_out", [4 * H, NTOK], BF16).ap()
    rs_in = nc.dram_tensor("rs_in", [S, H], BF16).ap()
    rs_out = nc.dram_tensor("rs_out", [NTOK, H], BF16).ap()

    with contextlib.ExitStack() as ctx:
        E = ctx.enter_context

        xres = E(nc.sbuf_tensor("xres", [128, 8 * H], F32))
        scal = E(nc.sbuf_tensor("scal", [128, 128], F32))
        wsv = E(nc.sbuf_tensor("wsv", [128, 4], F32))
        rwsv = E(nc.sbuf_tensor("rwsv", [128, 4], F32))
        ident = E(nc.sbuf_tensor("identb", [128, 128], BF16))
        onesb = E(nc.sbuf_tensor("onesb", [128, 128], BF16))

        sx = Sem(ctx, nc, "sx")
        sg = Sem(ctx, nc, "sg")
        scc = Sem(ctx, nc, "scc")
        sv = Sem(ctx, nc, "sv")
        sp = Sem(ctx, nc, "sp")
        sa = Sem(ctx, nc, "sa")
        sw32 = SlotSems(ctx, nc, "sw32_", 2)
        swl = SlotSems(ctx, nc, "swl_", 2)
        swdl = SlotSems(ctx, nc, "swdl_", 2)
        sstp = SlotSems(ctx, nc, "sstp_", 2)
        swq = Sem(ctx, nc, "swq")
        swqp = Sem(ctx, nc, "swqp")
        ss32f = Sem(ctx, nc, "ss32f")
        spsA = Sem(ctx, nc, "spsA")
        sevD = Sem(ctx, nc, "sevD")
        swdf = Sem(ctx, nc, "swdf")
        sout = Sem(ctx, nc, "sout")
        if os.environ.get("COLDPROF"):
            _pert = Sem(ctx, nc, "pert%d" % int(os.environ.get("PERT", "0")))

        def ve(instr):
            instr.then_inc(sv.h, 1)
            sv.n += 1
            nc.vector.wait_ge(sv.h, sv.n)

        def wait_all_stash(eng):
            eng.wait_ge(sstp.h(0), sstp.count(0))
            eng.wait_ge(sstp.h(1), sstp.count(1))

        nc.sync.dma_start(out=xres.ap().rearrange("p (t h) -> p t h", h=H),
                          in_=xin.rearrange("(t p) h -> p t h", p=128)).then_inc(sx.h, 16)
        sx.n += 16
        nc.sync.dma_start(out=ident[:, :], in_=ident_in).then_inc(sx.h, 16)
        sx.n += 16
        nc.vector.memset(onesb[:, :], 1.0)
        nc.vector.memset(scal[:, 124:125], EPS)

        with nc.sbuf_tensor("wstage", [128, 2 * 4096], F32) as wstage, \
             nc.sbuf_tensor("pslots", [128, 96], F32) as pslots, \
             nc.sbuf_tensor("onesf", [128, 128], F32) as onesf, \
             nc.sbuf_tensor("gustage", [128, IF], BF16) as gustage, \
             nc.sbuf_tensor("i32p", [128, 4096], I32) as i32p, \
             nc.psum_tensor("psws", [128, 4], F32) as psws:
            nc.vector.memset(onesf[:, :], 1.0)
            gv = wshg.rearrange("(i p) o -> p i o", p=128)
            uv = wshu.rearrange("(i p) o -> p i o", p=128)
            dv = wshd.rearrange("(i p) o -> p i o", p=128)
            pieces = ([("g", gv[:, 4 * j:4 * j + 4, :]) for j in range(4)]
                      + [("u", uv[:, 4 * j:4 * j + 4, :]) for j in range(4)]
                      + [("d", dv[:, 2 * j:2 * j + 2, :]) for j in range(4)])
            sv_base = sv.n
            for pi, (wname, ap_) in enumerate(pieces):
                slot = pi % 2
                nc.sync.wait_ge(sv.h, sv_base + max(0, pi - 1))
                ii = nc.sync.dma_start(out=wstage[:, slot * 4096:(slot + 1) * 4096], in_=ap_)
                sw32.inc16(ii, slot)
                nc.vector.wait_ge(sw32.h(slot), sw32.count(slot))
                nc.vector.tensor_reduce(out=pslots[:, 4 * pi:4 * pi + 1],
                                        in_=wstage[:, slot * 4096:(slot + 1) * 4096],
                                        axis=AX.X, op=ALU.add,
                                        apply_absolute_value=True).then_inc(sv.h, 1)
                sv.n += 1
            nc.vector.wait_ge(sv.h, sv.n)
            for w in range(3):
                ve(nc.vector.tensor_tensor(out=pslots[:, 64 + 8 * w:65 + 8 * w],
                                           in0=pslots[:, 16 * w:16 * w + 1],
                                           in1=pslots[:, 16 * w + 4:16 * w + 5], op=ALU.add))
                ve(nc.vector.tensor_tensor(out=pslots[:, 68 + 8 * w:69 + 8 * w],
                                           in0=pslots[:, 16 * w + 8:16 * w + 9],
                                           in1=pslots[:, 16 * w + 12:16 * w + 13], op=ALU.add))
                ve(nc.vector.tensor_tensor(out=pslots[:, 48 + 4 * w:49 + 4 * w],
                                           in0=pslots[:, 64 + 8 * w:65 + 8 * w],
                                           in1=pslots[:, 68 + 8 * w:69 + 8 * w], op=ALU.add))
            nc.vector.memset(pslots[:, 60:61], 0.0).then_inc(sv.h, 1)
            sv.n += 1
            nc.tensor.wait_ge(sv.h, sv.n)
            nc.tensor.matmul(psws[:, :], lhsT=onesf[:, :],
                             rhs=pslots.ap().rearrange("p (a b) -> p a b", b=4)[:, 12:16, 0:1],
                             start=True, stop=True).then_inc(sp.h, 1)
            sp.n += 1
            nc.vector.wait_ge(sp.h, sp.n)
            nc.vector.tensor_copy(out=wsv[:, :], in_=psws[:, :]).then_inc(sv.h, 1)
            sv.n += 1
            nc.sync.wait_ge(sv.h, sv.n)
            nc.sync.dma_start(out=ws_in, in_=wsv[:, :]).then_inc(sx.h, 16)
            sx.n += 16
            nc.gpsimd.wait_ge(sx.h, sx.n)
            nc.gpsimd.collective_compute("AllReduce", ALU.add, replica_groups=[list(range(8))],
                                         ins=[ws_in.opt()], outs=[ws_out.opt()]).then_inc(scc.h, 1)
            scc.n += 1
            nc.gpsimd.wait_ge(scc.h, scc.n)
            nc.gpsimd.dma_start(out=wsv[:, :], in_=ws_out).then_inc(sg.h, 16)
            sg.n += 16
            nc.vector.wait_ge(sg.h, sg.n)
            ve(nc.vector.tensor_scalar(out=wsv[:, :], in0=wsv[:, :], scalar1=1.0 / WN,
                                       scalar2=EPS, op0=ALU.mult, op1=ALU.add))
            nc.vector.reciprocal(out=rwsv[:, :], in_=wsv[:, :]).then_inc(sv.h, 1)
            sv.n += 1
            nc.gpsimd.wait_ge(sv.h, sv.n)

            sv_t0 = sv.n
            for j in range(4):
                for wi, wv_ in enumerate((gv, uv)):
                    slot = (2 * j + wi) % 2
                    nc.sync.wait_ge(sv.h, sv_t0 + max(0, 2 * j + wi - 1))
                    ii = nc.sync.dma_start(out=wstage[:, slot * 4096:(slot + 1) * 4096],
                                           in_=wv_[:, :, j * 256:(j + 1) * 256])
                    sw32.inc16(ii, slot)
                    nc.vector.wait_ge(sw32.h(slot), sw32.count(slot))
                    nc.vector.tensor_scalar(out=i32p[:, :], in0=wstage[:, slot * 4096:(slot + 1) * 4096],
                                            scalar1=rwsv[:, wi:wi + 1], scalar2=0.0,
                                            op0=ALU.mult, op1=ALU.add)
                    dstgu = gustage.ap().rearrange("p (i c) -> p i c", i=16)[
                        :, :, wi * 256:(wi + 1) * 256]
                    nc.vector.tensor_scalar(out=dstgu, in0=i32p[:, :], scalar1=1.0,
                                            scalar2=-1.0, op0=ALU.min, op1=ALU.max).then_inc(sv.h, 1)
                    sv.n += 1
                nc.sync.wait_ge(sv.h, sv.n)
                ii = nc.sync.dma_start(out=gu_in[j], in_=gustage[:, :])
                sstp.inc16(ii, 0)
                nc.gpsimd.wait_ge(sstp.h(0), sstp.count(0))
                nc.gpsimd.collective_compute("AllGather", ALU.bypass, replica_groups=[list(range(8))],
                                             ins=[gu_in[j].opt()],
                                             outs=[stash_gu[j * 1024:(j + 1) * 1024, :].opt()]).then_inc(scc.h, 1)
                scc.n += 1
            sv_t1 = sv.n
            for j in range(2):
                for k in range(4):
                    slot = (4 * j + k) % 2
                    nc.sync.wait_ge(sv.h, sv_t1 + max(0, 4 * j + k - 1))
                    ii = nc.sync.dma_start(out=wstage[:, slot * 4096:slot * 4096 + 2048],
                                           in_=dv[:, 4 * j + k, :])
                    sw32.inc16(ii, slot)
                    nc.vector.wait_ge(sw32.h(slot), sw32.count(slot))
                    nc.vector.tensor_scalar(out=i32p[:, 0:2048],
                                            in0=wstage[:, slot * 4096:slot * 4096 + 2048],
                                            scalar1=rwsv[:, 2:3], scalar2=0.0,
                                            op0=ALU.mult, op1=ALU.add)
                    nc.vector.tensor_scalar(out=gustage[:, k * 2048:(k + 1) * 2048],
                                            in0=i32p[:, 0:2048], scalar1=1.0,
                                            scalar2=-1.0, op0=ALU.min, op1=ALU.max).then_inc(sv.h, 1)
                    sv.n += 1
                nc.sync.wait_ge(sv.h, sv.n)
                ii = nc.sync.dma_start(out=d_in[j].rearrange("(k p) c -> p k c", p=128),
                                       in_=gustage.ap().rearrange("p (k c) -> p k c", k=4))
                sstp.inc16(ii, 0)
                nc.gpsimd.wait_ge(sstp.h(0), sstp.count(0))
                nc.gpsimd.collective_compute("AllGather", ALU.bypass, replica_groups=[list(range(8))],
                                             ins=[d_in[j].opt()],
                                             outs=[stash_d[j * 4096:(j + 1) * 4096, :].opt()]).then_inc(scc.h, 1)
                scc.n += 1
        gu_gather_done = [scc.n - 5, scc.n - 4, scc.n - 3, scc.n - 2]
        d_gather_done = [scc.n - 1, scc.n]

        nc.vector.wait_ge(sx.h, sx.n)

        def ln_quant_tiles(tiles, xqT, mv, st6, xqbf, i32scr, ptrans, ev32):
            ptr_banks = ptrans.ap().rearrange("p (b c) -> p b c", b=8)
            for j, t in enumerate(tiles):
                xt = xres[:, t * H:(t + 1) * H]
                for k in range(4):
                    bs = nc.vector.bn_stats(out=st6[:, 6 * k:6 * k + 6], in_=xt[:, 512 * k:512 * (k + 1)])
                ve(bs)
                ve(nc.vector.bn_aggr(out=mv[:, 0:2],
                                     in_=st6.ap().rearrange("p (n t) -> p n t", t=6)))
                nc.vector.tensor_copy(out=mv[:, 4:5], in_=mv[:, 1:2]).then_inc(sv.h, 1)
                sv.n += 1
                nc.scalar.wait_ge(sv.h, sv.n)
                nc.scalar.activation(out=mv[:, 8:9], in_=mv[:, 4:5], func=AF.Sqrt,
                                     bias=scal[:, 124:125]).then_inc(sa.h, 1)
                sa.n += 1
                nc.vector.wait_ge(sa.h, sa.n)
                nc.vector.reciprocal(out=mv[:, 12:13], in_=mv[:, 8:9])
                nc.vector.reduce_max(out=mv[:, 16:17], in_=xt, axis=AX.X)
                ve(nc.vector.tensor_reduce(out=mv[:, 20:21], in_=xt, axis=AX.X, op=ALU.min))
                nc.vector.tensor_tensor(out=mv[:, 24:25], in0=mv[:, 16:17], in1=mv[:, 0:1], op=ALU.subtract)
                ve(nc.vector.tensor_tensor(out=mv[:, 28:29], in0=mv[:, 0:1], in1=mv[:, 20:21], op=ALU.subtract))
                ve(nc.vector.tensor_tensor(out=mv[:, 24:25], in0=mv[:, 24:25], in1=mv[:, 28:29], op=ALU.max))
                ve(nc.vector.tensor_scalar(out=mv[:, 32:33], in0=mv[:, 24:25], scalar1=mv[:, 12:13],
                                           scalar2=1.0 / 127.0, op0=ALU.mult, op1=ALU.mult))
                ve(nc.vector.tensor_scalar(out=mv[:, 32:33], in0=mv[:, 32:33], scalar1=EPS,
                                           scalar2=None, op0=ALU.add))
                ve(nc.vector.reciprocal(out=mv[:, 36:37], in_=mv[:, 32:33]))
                ve(nc.vector.tensor_scalar(out=mv[:, 40:41], in0=mv[:, 12:13], scalar1=mv[:, 36:37],
                                           scalar2=None, op0=ALU.mult))
                ve(nc.vector.tensor_scalar(out=mv[:, 44:45], in0=mv[:, 0:1], scalar1=mv[:, 40:41],
                                           scalar2=-1.0, op0=ALU.mult, op1=ALU.mult))
                nc.vector.tensor_scalar(out=scal[:, 4 * t:4 * t + 1], in0=mv[:, 32:33],
                                        scalar1=wsv[:, 0:1], scalar2=None, op0=ALU.mult)
                nc.vector.tensor_scalar(out=scal[:, 32 + 4 * t:33 + 4 * t], in0=mv[:, 32:33],
                                        scalar1=wsv[:, 1:2], scalar2=None, op0=ALU.mult)
                nc.vector.tensor_scalar(out=i32scr[:, :], in0=xt, scalar1=mv[:, 40:41],
                                        scalar2=mv[:, 44:45], op0=ALU.mult, op1=ALU.add)
                nc.vector.tensor_copy(out=xqbf[:, :], in_=i32scr[:, :]).then_inc(sv.h, 1)
                sv.n += 1
                nc.tensor.wait_ge(sv.h, sv.n)
                for half in range(2):
                    for i in range(8):
                        tr = nc.tensor.transpose(
                            ptrans[:, i * 1024: i * 1024 + 128],
                            xqbf[:, (half * 8 + i) * 128:(half * 8 + i + 1) * 128],
                            ident[:, :])
                    tr.then_inc(sp.h, 1)
                    sp.n += 1
                    nc.vector.wait_ge(sp.h, sp.n)
                    nc.vector.tensor_copy(out=ev32[:, :], in_=ptr_banks[:, :, 0:128])
                    nc.vector.tensor_copy(
                        out=xqT[:, j * H + half * 1024: j * H + (half + 1) * 1024],
                        in_=ev32[:, :]).then_inc(sv.h, 1)
                    sv.n += 1
                    nc.tensor.wait_ge(sv.h, sv.n)

        def ffn(inv, grp, tiles, bufs):
            (xqT, st6, mv, i32scr, g2buf, gslots, ev32) = bufs

            with nc.psum_tensor("ptr%d%d" % (inv, grp), [128, 8192], BF16) as ptrans, \
                 nc.sbuf_tensor("xqbf%d%d" % (inv, grp), [128, H], BF16) as xqbf:
                nc.vector.wait_ge(swdf.h, swdf.n)
                wait_all_stash(nc.vector)
                ln_quant_tiles(tiles, xqT, mv, st6, xqbf, i32scr, ptrans, ev32)

            with nc.psum_tensor("psA%d%d" % (inv, grp), [128, 4096], F32) as psA, \
                 nc.sbuf_tensor("wqbuf%d%d" % (inv, grp), [128, 2 * IF], BF16) as wqbuf, \
                 nc.sbuf_tensor("gatec%d%d" % (inv, grp), [128, 8 * 256], BF16) as gatec:
                nc.sync.wait_ge(sv.h, sv.n)
                wait_all_stash(nc.sync)
                nc.sync.wait_ge(sp.h, sp.n)
                nc.sync.wait_ge(swdf.h, swdf.n)
                nc.sync.wait_ge(spsA.h, spsA.n)
                nc.gpsimd.wait_ge(swdf.h, swdf.n)
                nc.gpsimd.wait_ge(spsA.h, spsA.n)

                spsA0, sevD0 = spsA.n, sevD.n
                for oc in range(32):
                    par = oc % 2
                    nc.sync.wait_ge(scc.h, gu_gather_done[oc // 8])
                    if oc < 2:
                        wait_all_stash(nc.sync)
                    nc.sync.wait_ge(spsA.h, spsA0 + max(0, (oc - 1) * 4))
                    ii = nc.sync.dma_start(out=wqbuf[:, par * IF:(par + 1) * IF],
                                           in_=stash_gu[oc * 128:(oc + 1) * 128, :])
                    swl.inc16(ii, par)

                    nc.tensor.wait_ge(swl.h(par), swl.count(par))
                    for j in range(4):
                        if oc >= 2:
                            nc.tensor.wait_ge(sevD.h, sevD0 + (oc - 2) * 4 + j + 1)
                        for i in range(16):
                            mm = nc.tensor.matmul(
                                psA[:, par * 2048 + j * 512: par * 2048 + (j + 1) * 512],
                                lhsT=xqT[:, j * H + i * 128: j * H + (i + 1) * 128],
                                rhs=wqbuf[:, par * IF + i * 512: par * IF + (i + 1) * 512],
                                start=(i == 0), stop=(i == 15))
                        mm.then_inc(spsA.h, 1)
                        spsA.n += 1

                    for j in range(4):
                        t = tiles[j]
                        nc.scalar.wait_ge(spsA.h, spsA0 + oc * 4 + j + 1)
                        nc.scalar.activation(
                            out=gatec[:, (par * 4 + j) * 256:(par * 4 + j + 1) * 256],
                            in_=psA[:, par * 2048 + j * 512: par * 2048 + j * 512 + 256],
                            func=AF.Silu, scale=scal[:, 4 * t:4 * t + 1]).then_inc(sa.h, 1)
                        sa.n += 1
                        nc.vector.wait_ge(sa.h, sa.n)
                        nc.vector.scalar_tensor_tensor(
                            out=g2buf[:, j * IF + oc * 256: j * IF + (oc + 1) * 256],
                            in0=psA[:, par * 2048 + j * 512 + 256: par * 2048 + (j + 1) * 512],
                            scalar=scal[:, 32 + 4 * t:33 + 4 * t],
                            in1=gatec[:, (par * 4 + j) * 256:(par * 4 + j + 1) * 256],
                            op0=ALU.mult, op1=ALU.mult).then_inc(sevD.h, 1)
                        sevD.n += 1
                        gm = nc.vector.reduce_max(out=gslots[:, j * 32 + oc:j * 32 + oc + 1],
                                                  in_=g2buf[:, j * IF + oc * 256: j * IF + (oc + 1) * 256],
                                                  axis=AX.X, apply_absolute_value=True)
                ve(gm)
                nc.tensor.wait_ge(sevD.h, sevD0 + 128)

            for j, t in enumerate(tiles):
                ve(nc.vector.tensor_reduce(out=mv[:, 0:1], in_=gslots[:, j * 32:(j + 1) * 32],
                                           axis=AX.X, op=ALU.max))
                ve(nc.vector.tensor_scalar(out=mv[:, 4:5], in0=mv[:, 0:1], scalar1=1.0 / 127.0,
                                           scalar2=EPS, op0=ALU.mult, op1=ALU.add))
                ve(nc.vector.reciprocal(out=mv[:, 8:9], in_=mv[:, 4:5]))
                nc.vector.tensor_scalar(out=scal[:, 64 + 4 * t:65 + 4 * t], in0=mv[:, 4:5],
                                        scalar1=wsv[:, 2:3], scalar2=0.5,
                                        op0=ALU.mult, op1=ALU.mult)
                for c in range(4):
                    sl = g2buf[:, j * IF + c * 2048: j * IF + (c + 1) * 2048]
                    nc.vector.tensor_scalar(out=i32scr[:, :], in0=sl, scalar1=mv[:, 8:9],
                                            scalar2=None, op0=ALU.mult)
                    cpy = nc.vector.tensor_copy(out=sl, in_=i32scr[:, :])
            cpy.then_inc(sv.h, 1)
            sv.n += 1
            g2q_done = sv.n

            for tp in range(2):
                pj = [2 * tp, 2 * tp + 1]
                with nc.sbuf_tensor("g2qT%d%d%d" % (inv, grp, tp), [128, 2 * IF], BF16) as g2qT:
                    with nc.psum_tensor("ptrD%d%d%d" % (inv, grp, tp), [128, 8192], BF16) as ptrD:
                        wait_all_stash(nc.vector)
                        nc.tensor.wait_ge(sv.h, g2q_done)
                        ptrD_banks = ptrD.ap().rearrange("p (b c) -> p b c", b=8)
                        for jj, j in enumerate(pj):
                            for half in range(8):
                                for i in range(8):
                                    i2 = half * 8 + i
                                    tr = nc.tensor.transpose(
                                        ptrD[:, i * 1024: i * 1024 + 128],
                                        g2buf[:, j * IF + i2 * 128: j * IF + (i2 + 1) * 128],
                                        ident[:, :])
                                tr.then_inc(sp.h, 1)
                                sp.n += 1
                                nc.vector.wait_ge(sp.h, sp.n)
                                nc.vector.tensor_copy(out=ev32[:, :], in_=ptrD_banks[:, :, 0:128])
                                nc.vector.tensor_copy(
                                    out=g2qT[:, jj * IF + half * 1024: jj * IF + (half + 1) * 1024],
                                    in_=ev32[:, :]).then_inc(sv.h, 1)
                                sv.n += 1
                                nc.tensor.wait_ge(sv.h, sv.n)

                    with nc.psum_tensor("psD%d%d%d" % (inv, grp, tp), [128, 4096], F32) as psD, \
                         nc.sbuf_tensor("wdc%d%d%d" % (inv, grp, tp), [128, 2 * H], BF16) as wdchunk:
                        nc.sync.wait_ge(sv.h, sv.n)
                        wait_all_stash(nc.sync)
                        nc.sync.wait_ge(spsA.h, spsA.n)
                        swdf0 = swdf.n
                        for i2 in range(I2C):
                            par = i2 % 2
                            nc.sync.wait_ge(scc.h, d_gather_done[i2 // 32])
                            if i2 < 2:
                                wait_all_stash(nc.sync)
                            nc.sync.wait_ge(swdf.h, swdf0 + max(0, i2 - 1))
                            ii = nc.sync.dma_start(out=wdchunk[:, par * H:(par + 1) * H],
                                                   in_=stash_d[i2 * 128:(i2 + 1) * 128, :])
                            swdl.inc16(ii, par)
                            nc.tensor.wait_ge(swdl.h(par), swdl.count(par))
                            for jj in range(2):
                                for oc in range(4):
                                    mm = nc.tensor.matmul(
                                        psD[:, jj * 2048 + oc * 512: jj * 2048 + (oc + 1) * 512],
                                        lhsT=g2qT[:, jj * IF + i2 * 128: jj * IF + (i2 + 1) * 128],
                                        rhs=wdchunk[:, par * H + oc * 512: par * H + (oc + 1) * 512],
                                        start=(i2 == 0), stop=(i2 == I2C - 1))
                            mm.then_inc(swdf.h, 1)
                            swdf.n += 1
                        nc.vector.wait_ge(swdf.h, swdf0 + I2C)
                        for jj, j in enumerate(pj):
                            t = tiles[j]
                            for oc in range(4):
                                stt = nc.vector.scalar_tensor_tensor(
                                    out=xres[:, t * H + oc * 512: t * H + (oc + 1) * 512],
                                    in0=psD[:, jj * 2048 + oc * 512: jj * 2048 + (oc + 1) * 512],
                                    scalar=scal[:, 64 + 4 * t:65 + 4 * t],
                                    in1=xres[:, t * H + oc * 512: t * H + (oc + 1) * 512],
                                    op0=ALU.mult, op1=ALU.add)
                        stt.then_inc(sv.h, 1)
                        sv.n += 1
                        nc.tensor.wait_ge(sv.h, sv.n)


        def attention(st6, mv, ev32):
            nc.sync.wait_ge(sv.h, sv.n)
            nc.sync.wait_ge(sp.h, sp.n)
            nc.sync.wait_ge(spsA.h, spsA.n)
            nc.sync.wait_ge(swdf.h, swdf.n)
            wait_all_stash(nc.sync)
            nc.gpsimd.wait_ge(sv.h, sv.n)
            nc.gpsimd.wait_ge(swdf.h, swdf.n)
            with contextlib.ExitStack() as actx:
                AE = actx.enter_context
                masks_sb = AE(nc.sbuf_tensor("masksb", [128, 2048], BF16))
                wd_sb = AE(nc.sbuf_tensor("wdsb", [128, 4 * 2048], BF16))
                QT = AE(nc.sbuf_tensor("QTs", [128, 4 * 4096], BF16))
                KT = AE(nc.sbuf_tensor("KTs", [128, 4096], BF16))
                Vs = AE(nc.sbuf_tensor("Vss", [128, 4096], BF16))
                nc.sync.dma_start(out=masks_sb.ap().rearrange("p (m j) -> p m j", m=2),
                                  in_=masks_in.rearrange("(m p) j -> p m j", p=128)).then_inc(sx.h, 16)
                sx.n += 16
                nc.gpsimd.dma_start(out=wd_sb.ap().rearrange("p (e h) -> p e h", e=4),
                                    in_=wd_s.rearrange("(e p) h -> p e h", p=128)).then_inc(sg.h, 16)
                sg.n += 16

                with nc.sbuf_tensor("z2T", [128, 16 * NTOK], BF16) as z2T, \
                     nc.sbuf_tensor("z2bf", [128, H], BF16) as z2bf, \
                     nc.psum_tensor("ptrZ", [128, 8192], BF16) as ptrZ:
                    ptrZ_banks = ptrZ.ap().rearrange("p (b c) -> p b c", b=8)
                    for t in range(8):
                        xt = xres[:, t * H:(t + 1) * H]
                        for k in range(4):
                            bs = nc.vector.bn_stats(out=st6[:, 6 * k:6 * k + 6],
                                                    in_=xt[:, 512 * k:512 * (k + 1)])
                        ve(bs)
                        ve(nc.vector.bn_aggr(out=mv[:, 0:2],
                                             in_=st6.ap().rearrange("p (n t) -> p n t", t=6)))
                        nc.vector.tensor_copy(out=mv[:, 4:5], in_=mv[:, 1:2]).then_inc(sv.h, 1)
                        sv.n += 1
                        nc.scalar.wait_ge(sv.h, sv.n)
                        nc.scalar.activation(out=mv[:, 8:9], in_=mv[:, 4:5], func=AF.Sqrt,
                                             bias=scal[:, 124:125]).then_inc(sa.h, 1)
                        sa.n += 1
                        nc.vector.wait_ge(sa.h, sa.n)
                        ve(nc.vector.reciprocal(out=mv[:, 12:13], in_=mv[:, 8:9]))
                        nc.vector.tensor_scalar(out=mv[:, 44:45], in0=mv[:, 0:1],
                                                scalar1=mv[:, 12:13], scalar2=-1.0,
                                                op0=ALU.mult, op1=ALU.mult).then_inc(sv.h, 1)
                        sv.n += 1
                        nc.scalar.wait_ge(sv.h, sv.n)
                        nc.scalar.activation(out=z2bf[:, :], in_=xt, func=AF.Identity,
                                             scale=mv[:, 12:13], bias=mv[:, 44:45]).then_inc(sa.h, 1)
                        sa.n += 1
                        nc.tensor.wait_ge(sa.h, sa.n)
                        for half in range(2):
                            for i in range(8):
                                tr = nc.tensor.transpose(
                                    ptrZ[:, i * 1024: i * 1024 + 128],
                                    z2bf[:, (half * 8 + i) * 128:(half * 8 + i + 1) * 128],
                                    ident[:, :])
                            tr.then_inc(sp.h, 1)
                            sp.n += 1
                            nc.vector.wait_ge(sp.h, sp.n)
                            nc.vector.tensor_copy(out=ev32[:, :], in_=ptrZ_banks[:, :, 0:128])
                            nc.vector.tensor_copy(
                                out=z2T[:, t * 2048 + half * 1024: t * 2048 + (half + 1) * 1024],
                                in_=ev32[:, :]).then_inc(sv.h, 1)
                            sv.n += 1
                            nc.tensor.wait_ge(sv.h, sv.n)
                    nc.sync.wait_ge(sv.h, sv.n)
                    ii = nc.sync.dma_start(
                        out=z2t_in.rearrange("(hc p) (t j) -> p t hc j", p=128, j=128),
                        in_=z2T.ap().rearrange("p (t hc j) -> p t hc j", hc=16, j=128))
                    sstp.inc16(ii, 0)
                    nc.gpsimd.wait_ge(sstp.h(0), sstp.count(0))
                    nc.gpsimd.collective_compute(
                        "AllGather", ALU.bypass, replica_groups=[[0, 1, 2, 3], [4, 5, 6, 7]],
                        ins=[z2t_in.opt()], outs=[z2t_out.opt()]).then_inc(scc.h, 1)
                    scc.n += 1
                z2_done = scc.n

                spro0 = spsA.n
                with nc.sbuf_tensor("ztb", [128, 16 * 512], BF16) as ztb, \
                     nc.sbuf_tensor("wqsb", [128, 16 * 512], BF16) as wqsb, \
                     nc.sbuf_tensor("wksb", [128, 16 * 128], BF16) as wksb, \
                     nc.sbuf_tensor("wvsb", [128, 16 * 128], BF16) as wvsb, \
                     nc.sbuf_tensor("VTs", [128, 4096], BF16) as VTs:
                    with nc.psum_tensor("psP", [128, 4096], F32) as psP:
                        nc.gpsimd.dma_start(out=wqsb.ap().rearrange("p (hc e) -> p hc e", hc=16),
                                            in_=wq_s.rearrange("(hc p) e -> p hc e", p=128)).then_inc(sg.h, 16)
                        sg.n += 16
                        nc.gpsimd.dma_start(out=wksb.ap().rearrange("p (hc e) -> p hc e", hc=16),
                                            in_=wk_s.rearrange("(hc p) e -> p hc e", p=128)).then_inc(sg.h, 16)
                        sg.n += 16
                        nc.gpsimd.dma_start(out=wvsb.ap().rearrange("p (hc e) -> p hc e", hc=16),
                                            in_=wv_s.rearrange("(hc p) e -> p hc e", p=128)).then_inc(sg.h, 16)
                        sg.n += 16
                        for T in range(8):
                            r, col0 = T // 2, (T % 2) * 512
                            nc.sync.wait_ge(scc.h, z2_done)
                            nc.sync.wait_ge(spsA.h, spro0 + T)
                            ii = nc.sync.dma_start(
                                out=ztb.ap().rearrange("p (hc j) -> p hc j", hc=16),
                                in_=z2t_out[r * 2048:(r + 1) * 2048, col0:col0 + 512].rearrange(
                                    "(hc p) j -> p hc j", p=128))
                            swl.inc16(ii, T % 2)
                            nc.tensor.wait_ge(swl.h(T % 2), swl.count(T % 2))
                            if T == 0:
                                nc.tensor.wait_ge(sg.h, sg.n)
                            for hc in range(16):
                                for e in range(4):
                                    nc.tensor.matmul(psP[:, e * 512:(e + 1) * 512],
                                                     lhsT=wqsb[:, hc * 512 + e * 128: hc * 512 + (e + 1) * 128],
                                                     rhs=ztb[:, hc * 512:(hc + 1) * 512],
                                                     start=(hc == 0), stop=(hc == 15))
                                nc.tensor.matmul(psP[:, 2048:2560],
                                                 lhsT=wksb[:, hc * 128:(hc + 1) * 128],
                                                 rhs=ztb[:, hc * 512:(hc + 1) * 512],
                                                 start=(hc == 0), stop=(hc == 15))
                                mm = nc.tensor.matmul(psP[:, 2560:3072],
                                                      lhsT=wvsb[:, hc * 128:(hc + 1) * 128],
                                                      rhs=ztb[:, hc * 512:(hc + 1) * 512],
                                                      start=(hc == 0), stop=(hc == 15))
                            mm.then_inc(spsA.h, 1)
                            spsA.n += 1
                            nc.scalar.wait_ge(spsA.h, spsA.n)
                            for e in range(4):
                                nc.scalar.activation(out=QT[:, e * 4096 + T * 512:e * 4096 + (T + 1) * 512],
                                                     in_=psP[:, e * 512:(e + 1) * 512], func=AF.Identity)
                            nc.scalar.activation(out=KT[:, T * 512:(T + 1) * 512],
                                                 in_=psP[:, 2048:2560], func=AF.Identity)
                            nc.scalar.activation(out=VTs[:, T * 512:(T + 1) * 512],
                                                 in_=psP[:, 2560:3072], func=AF.Identity).then_inc(sa.h, 1)
                            sa.n += 1
                            nc.tensor.wait_ge(sa.h, sa.n)
                    with nc.psum_tensor("ptrV", [128, 8192], BF16) as ptrV:
                        ptrV_banks = ptrV.ap().rearrange("p (b c) -> p b c", b=8)
                        for rnd in range(4):
                            for i in range(8):
                                kc = rnd * 8 + i
                                tr = nc.tensor.transpose(ptrV[:, i * 1024: i * 1024 + 128],
                                                         VTs[:, kc * 128:(kc + 1) * 128],
                                                         ident[:, :])
                            tr.then_inc(sp.h, 1)
                            sp.n += 1
                            nc.vector.wait_ge(sp.h, sp.n)
                            nc.vector.tensor_copy(out=ev32[:, :], in_=ptrV_banks[:, :, 0:128])
                            nc.vector.tensor_copy(out=Vs[:, rnd * 1024:(rnd + 1) * 1024],
                                                  in_=ev32[:, :]).then_inc(sv.h, 1)
                            sv.n += 1
                            nc.tensor.wait_ge(sv.h, sv.n)

                nc.gpsimd.wait_ge(sv.h, sv.n)
                nc.gpsimd.dma_start(out=dbg2[:, 0:1024], in_=KT[:, 0:1024]).then_inc(sout.h, 16)
                sout.n += 16
                nc.gpsimd.dma_start(out=dbg2[:, 1024:2048], in_=Vs[:, 0:1024]).then_inc(sout.h, 16)
                sout.n += 16
                nc.gpsimd.dma_start(out=dbg2[:, 2048:3072], in_=QT[:, 0:1024]).then_inc(sout.h, 16)
                sout.n += 16
                with nc.sbuf_tensor("ctxT", [128, 4 * 4096], BF16) as ctxT, \
                     nc.sbuf_tensor("Esb", [128, 2 * 1024], BF16) as Esb, \
                     nc.sbuf_tensor("rcp", [128, 1024], F32) as rcp, \
                     nc.sbuf_tensor("dnsb", [128, 2048], BF16) as dnsb, \
                     nc.psum_tensor("pS", [128, 1024], F32) as pS, \
                     nc.psum_tensor("pC", [128, 2048], F32) as pC, \
                     nc.psum_tensor("pDen", [128, 1024], F32) as pDen:
                    for qb in range(16):
                        nkc = 2 * qb + 2
                        sa_q0 = sa.n
                        for kc in range(nkc):
                            par = kc % 2
                            if kc >= 1:
                                nc.tensor.wait_ge(sa.h, sa_q0 + kc)
                            for h in range(4):
                                mm = nc.tensor.matmul(
                                    pS[:, h * 256:(h + 1) * 256],
                                    lhsT=KT[:, kc * 128:(kc + 1) * 128],
                                    rhs=QT[:, h * 4096 + qb * 256: h * 4096 + (qb + 1) * 256],
                                    start=True, stop=True)
                            mm.then_inc(spsA.h, 1)
                            spsA.n += 1
                            nc.scalar.wait_ge(spsA.h, spsA.n)
                            nc.scalar.activation(out=Esb[:, par * 1024:(par + 1) * 1024],
                                                 in_=pS[:, :],
                                                 func=AF.Exp, scale=INV_SQRT_D).then_inc(sa.h, 1)
                            sa.n += 1
                            diag = (kc >= nkc - 2)
                            if diag:
                                nc.vector.wait_ge(sa.h, sa.n)
                                nc.vector.tensor_tensor(
                                    out=Esb[:, par * 1024:(par + 1) * 1024],
                                    in0=Esb[:, par * 1024:(par + 1) * 1024],
                                    in1=masks_sb[:, (kc - (nkc - 2)) * 1024:(kc - (nkc - 2) + 1) * 1024],
                                    op=ALU.mult).then_inc(sv.h, 1)
                                sv.n += 1
                                nc.tensor.wait_ge(sv.h, sv.n)
                            else:
                                nc.tensor.wait_ge(sa.h, sa.n)
                            if kc == 0:
                                nc.tensor.wait_ge(sv.h, sv.n)
                            for h in range(4):
                                nc.tensor.matmul(pC[:, h * 512: h * 512 + 256],
                                                 lhsT=Vs[:, kc * 128:(kc + 1) * 128],
                                                 rhs=Esb[:, par * 1024 + h * 256: par * 1024 + (h + 1) * 256],
                                                 start=(kc == 0), stop=(kc == nkc - 1))
                            for dd in range(2):
                                mm = nc.tensor.matmul(
                                    pDen[:, dd * 512:(dd + 1) * 512],
                                    lhsT=onesb[:, :],
                                    rhs=Esb[:, par * 1024 + dd * 512: par * 1024 + (dd + 1) * 512],
                                    start=(kc == 0), stop=(kc == nkc - 1))
                            mm.then_inc(spsA.h, 1)
                            spsA.n += 1
                        nc.vector.wait_ge(spsA.h, spsA.n)
                        ve(nc.vector.reciprocal(out=rcp[:, :], in_=pDen[:, :]))
                        for h in range(4):
                            tt = nc.vector.tensor_tensor(
                                out=ctxT[:, qb * 1024 + h * 256: qb * 1024 + (h + 1) * 256],
                                in0=pC[:, h * 512: h * 512 + 256],
                                in1=rcp[:, h * 256:(h + 1) * 256], op=ALU.mult)
                        tt.then_inc(sv.h, 1)
                        sv.n += 1

                    nc.gpsimd.wait_ge(sv.h, sv.n)
                    nc.gpsimd.dma_start(out=dbg2[:, 3072:4096], in_=ctxT[:, 0:1024]).then_inc(sout.h, 16)
                    sout.n += 16
                    nc.tensor.wait_ge(sv.h, sv.n)
                    nc.tensor.wait_ge(sg.h, sg.n)
                    saD0 = sa.n
                    for Tt in range(32):
                        par = Tt % 2
                        if Tt >= 2:
                            nc.tensor.wait_ge(sa.h, saD0 + 2 * (Tt - 1))
                        for oc in range(4):
                            if oc == 2:
                                nc.tensor.wait_ge(sa.h, sa.n)
                            for h in range(4):
                                mm = nc.tensor.matmul(
                                    pC[:, par * 1024 + (oc % 2) * 512: par * 1024 + (oc % 2 + 1) * 512],
                                    lhsT=ctxT[:, (Tt // 2) * 1024 + h * 256 + (Tt % 2) * 128:
                                              (Tt // 2) * 1024 + h * 256 + (Tt % 2) * 128 + 128],
                                    rhs=wd_sb[:, h * 2048 + oc * 512: h * 2048 + (oc + 1) * 512],
                                    start=(h == 0), stop=(h == 3))
                            mm.then_inc(spsA.h, 1)
                            spsA.n += 1
                            nc.scalar.wait_ge(spsA.h, spsA.n)
                            if oc % 2 == 0:
                                nc.scalar.wait_ge(sstp.h(par), sstp.count(par))
                            aci = nc.scalar.activation(
                                out=dnsb[:, par * 1024 + (oc % 2) * 512: par * 1024 + (oc % 2 + 1) * 512],
                                in_=pC[:, par * 1024 + (oc % 2) * 512: par * 1024 + (oc % 2 + 1) * 512],
                                func=AF.Identity)
                            if oc % 2 == 1:
                                aci.then_inc(sa.h, 1)
                                sa.n += 1
                                nc.sync.wait_ge(sa.h, sa.n)
                                ii = nc.sync.dma_start(
                                    out=rs_in[Tt * 128:(Tt + 1) * 128, (oc // 2) * 1024:(oc // 2 + 1) * 1024],
                                    in_=dnsb[:, par * 1024:(par + 1) * 1024])
                                sstp.inc16(ii, par)
                    wait_all_stash(nc.gpsimd)
                    nc.gpsimd.collective_compute(
                        "ReduceScatter", ALU.add, replica_groups=[[0, 1, 2, 3], [4, 5, 6, 7]],
                        ins=[rs_in.opt()], outs=[rs_out.opt()]).then_inc(scc.h, 1)
                    scc.n += 1
                    nc.sync.wait_ge(scc.h, scc.n)
                    nc.gpsimd.wait_ge(scc.h, scc.n)
                    nc.gpsimd.dma_start(out=dbg2[:, 0:2048], in_=rs_in[0:128, :]).then_inc(sout.h, 16)
                    sout.n += 16
                    wait_all_stash(nc.sync)
                    svP6 = sv.n
                    for t in range(8):
                        par = t % 2
                        if t >= 2:
                            nc.sync.wait_ge(sv.h, svP6 + t - 1)
                        ii = nc.sync.dma_start(out=Esb[:, par * 1024:(par + 1) * 1024],
                                               in_=rs_out[t * 128:(t + 1) * 128, 0:1024])
                        swdl.inc16(ii, par)
                        ii = nc.sync.dma_start(out=dnsb[:, par * 1024:(par + 1) * 1024],
                                               in_=rs_out[t * 128:(t + 1) * 128, 1024:2048])
                        swdl.inc16(ii, par)
                        nc.vector.wait_ge(swdl.h(par), swdl.count(par))
                        nc.vector.tensor_tensor(out=xres[:, t * H:t * H + 1024],
                                                in0=xres[:, t * H:t * H + 1024],
                                                in1=Esb[:, par * 1024:(par + 1) * 1024], op=ALU.add)
                        nc.vector.tensor_tensor(out=xres[:, t * H + 1024:(t + 1) * H],
                                                in0=xres[:, t * H + 1024:(t + 1) * H],
                                                in1=dnsb[:, par * 1024:(par + 1) * 1024],
                                                op=ALU.add).then_inc(sv.h, 1)
                        sv.n += 1
                    nc.tensor.wait_ge(sv.h, sv.n)

        with nc.sbuf_tensor("st6", [128, 24], F32) as st6, \
             nc.sbuf_tensor("mvt", [128, 64], F32) as mv, \
             nc.sbuf_tensor("i32a", [128, H], I32) as i32scr, \
             nc.sbuf_tensor("ev32", [128, 1024], F32) as ev32:
            with nc.sbuf_tensor("xqT0", [128, 4 * H], BF16) as xqT, \
                 nc.sbuf_tensor("g2buf0", [128, 4 * IF], BF16) as g2buf, \
                 nc.sbuf_tensor("gslots0", [128, 4 * 32], F32) as gslots:
                bufs = (xqT, st6, mv, i32scr, g2buf, gslots, ev32)
                if KSTAGE >= 2:
                    ffn(0, 0, [0, 1, 2, 3], bufs)
                if KSTAGE >= 5:
                    ffn(0, 1, [4, 5, 6, 7], bufs)
            if MILESTONE >= 2 and KSTAGE >= 6:
                attention(st6, mv, ev32)
                with nc.sbuf_tensor("xqT1", [128, 4 * H], BF16) as xqT, \
                     nc.sbuf_tensor("g2buf1", [128, 4 * IF], BF16) as g2buf, \
                     nc.sbuf_tensor("gslots1", [128, 4 * 32], F32) as gslots:
                    bufs = (xqT, st6, mv, i32scr, g2buf, gslots, ev32)
                    ffn(1, 0, [0, 1, 2, 3], bufs)
                    ffn(1, 1, [4, 5, 6, 7], bufs)

        nc.sync.wait_ge(sv.h, sv.n)
        nc.sync.dma_start(out=dbg[:, 0:4], in_=wsv[:, :]).then_inc(sout.h, 16)
        sout.n += 16
        nc.sync.dma_start(out=dbg[:, 4:8], in_=rwsv[:, :]).then_inc(sout.h, 16)
        sout.n += 16
        nc.sync.dma_start(out=yout.rearrange("(t p) h -> p t h", p=128),
                          in_=xres.ap().rearrange("p (t h) -> p t h", h=H)).then_inc(sout.h, 16)
        sout.n += 16
        nc.sync.wait_ge(sout.h, sout.n)

    return nc


def kernel(L0_attn_x, L0_attn_w_q, L0_attn_w_kv, L0_attn_w_dense,
           L0_ffn_w_gate, L0_ffn_w_up, L0_ffn_w_down):
    x = np.ascontiguousarray(np.asarray(L0_attn_x, np.float32).reshape(B * S, H))
    w_q = np.asarray(L0_attn_w_q, np.float32)
    w_kv = np.asarray(L0_attn_w_kv, np.float32)
    w_d = np.asarray(L0_attn_w_dense, np.float32)
    wgT = np.ascontiguousarray(np.asarray(L0_ffn_w_gate, np.float32).T)
    wuT = np.ascontiguousarray(np.asarray(L0_ffn_w_up, np.float32).T)
    wdT = np.ascontiguousarray(np.asarray(L0_ffn_w_down, np.float32).T)
    ident = np.eye(128, dtype=ml_dtypes.bfloat16)

    jj = np.arange(256)[None, :]
    kk = np.arange(128)[:, None]
    mA = np.tile((jj >= kk), (1, 4))[:, :1024]
    mB = np.tile((jj >= 128 + kk), (1, 4))[:, :1024]
    masks = np.ascontiguousarray(np.concatenate([mA, mB], axis=0).astype(ml_dtypes.bfloat16))

    in_maps = []
    for c in range(NCORE):
        g = c % 4
        gu_cols = np.concatenate([np.arange((8 * j + c) * 256, (8 * j + c + 1) * 256)
                                  for j in range(4)])
        d_rows = np.concatenate([np.arange((32 * j + 4 * c) * 128, (32 * j + 4 * c + 4) * 128)
                                 for j in range(2)])
        in_maps.append({
            "xin": np.ascontiguousarray(x[c * NTOK:(c + 1) * NTOK]),
            "wshg": np.ascontiguousarray(wgT[:, gu_cols]),
            "wshu": np.ascontiguousarray(wuT[:, gu_cols]),
            "wshd": np.ascontiguousarray(wdT[d_rows, :]),
            "ident": ident,
            "wq_s": np.ascontiguousarray(w_q[:, g * 512:(g + 1) * 512]),
            "wk_s": np.ascontiguousarray(w_kv[:, g * 128:(g + 1) * 128]),
            "wv_s": np.ascontiguousarray(w_kv[:, 512 + g * 128: 512 + (g + 1) * 128]),
            "wd_s": np.ascontiguousarray(w_d[g * 512:(g + 1) * 512, :]),
            "masks": masks,
        })

    nc = build()
    res = run_bass_kernel_spmd(nc, in_maps, core_ids=list(range(NCORE)))
    out = np.concatenate([np.asarray(r["yout"]) for r in res.results], axis=0)
    return out.reshape(B, S, H).astype(np.float32)
